# version 04
# baseline (speedup 1.0000x reference)
import numpy as np
import ml_dtypes
from contextlib import ExitStack

import concourse.bass as bass
import concourse.mybir as mybir
from concourse.bass_utils import run_bass_kernel_spmd

F32 = mybir.dt.float32
BF16 = mybir.dt.bfloat16
I32 = mybir.dt.int32
AF = mybir.ActivationFunctionType
ALU = mybir.AluOpType
AX = mybir.AxisListType
BFNP = ml_dtypes.bfloat16

D = 1024
NH = 16
HD = 64
BLK = 128
GEVERY = 24
DFF = 4096
NE = 8
EPS = 1e-5


class Ctr:
    def __init__(self, stack, nc, name):
        self.h = stack.enter_context(nc.semaphore(name))
        self.n = 0


class Eng:
    def __init__(self, eng):
        self.e = eng
        self.seen = {}

    def wt(self, c, v):
        if v is None or v <= 0:
            return
        if self.seen.get(id(c), 0) >= v:
            return
        self.e.wait_ge(c.h, v)
        self.seen[id(c)] = v


def build(T=4096, dbg=None):
    NBLK = T // BLK
    GLB = list(range(0, NBLK, GEVERY))
    NCH = T // 512
    NS_AB = T // 512
    NS_MOE = T // 256

    nc = bass.Bass("TRN2", target_bir_lowering=False,
                   detect_race_conditions=False)
    st = ExitStack()

    x_in = nc.dram_tensor("x", [T, D], F32, kind="ExternalInput").ap()
    wqkvT_in = nc.dram_tensor("wqkvT", [D, 3 * D], F32, kind="ExternalInput").ap()
    bqkv_in = nc.dram_tensor("bqkv", [3 * D], F32, kind="ExternalInput").ap()
    woutT_in = nc.dram_tensor("woutT", [D, D], F32, kind="ExternalInput").ap()
    bout_in = nc.dram_tensor("bout", [D], F32, kind="ExternalInput").ap()
    W1_in = nc.dram_tensor("W1", [NE, D, DFF], F32, kind="ExternalInput").ap()
    b1_in = nc.dram_tensor("b1", [NE, DFF], F32, kind="ExternalInput").ap()
    W2_in = nc.dram_tensor("W2", [NE, DFF, D], F32, kind="ExternalInput").ap()
    b2_in = nc.dram_tensor("b2", [NE, D], F32, kind="ExternalInput").ap()
    eye_in = nc.dram_tensor("eye", [128, 128], BF16, kind="ExternalInput").ap()
    e8h_in = nc.dram_tensor("e8hot", [8, 8 * 128], BF16, kind="ExternalInput").ap()
    out_hbm = nc.dram_tensor("out", [T, D], F32, kind="ExternalOutput").ap()

    q_scr = nc.dram_tensor("q_scr", [NBLK, 128, 8, 128], BF16, kind="Internal").ap()
    v_scr = nc.dram_tensor("v_scr", [T, D], BF16, kind="Internal").ap()
    x1_scr = nc.dram_tensor("x1_scr", [T, D], F32, kind="Internal").ap()
    xn2T_scr = nc.dram_tensor("xn2T_scr", [8, 128, T], BF16, kind="Internal").ap()
    acc_scr = nc.dram_tensor("acc_scr", [T, D], F32, kind="Internal").ap()

    TE, VE, SE, SY, GE = (Eng(nc.tensor), Eng(nc.vector), Eng(nc.scalar),
                          Eng(nc.sync), Eng(nc.gpsimd))
    ALL_E = [TE, VE, SE, SY, GE]
    tC = Ctr(st, nc, "tC")
    vC = Ctr(st, nc, "vC")
    sC = Ctr(st, nc, "sC")
    bar = Ctr(st, nc, "bar")

    dma_ctrs = []

    def dctr(name):
        c = Ctr(st, nc, name)
        dma_ctrs.append(c)
        return c

    CL = dctr("CL")
    WQ = [dctr(f"WQ{i}") for i in range(2)]
    XA = [dctr(f"XA{i}") for i in range(2)]
    QW = [dctr(f"QW{i}") for i in range(2)]
    VW = [dctr(f"VW{i}") for i in range(2)]
    VL = dctr("VL")
    WO = [dctr(f"WO{i}") for i in range(2)]
    QL = [dctr(f"QL{i}") for i in range(2)]
    XD = [dctr(f"XD{i}") for i in range(2)]
    X1W = [dctr(f"X1W{i}") for i in range(2)]
    XTW = [dctr(f"XTW{i}") for i in range(2)]
    STG = [dctr(f"STG{i}") for i in range(2)]
    XS = [dctr(f"XS{i}") for i in range(2)]
    AL = dctr("AL")
    XLL = dctr("XLL")
    OW = [dctr(f"OW{i}") for i in range(2)]
    DBG = dctr("DBG")



    def T_(fn, *a, w=(), **k):
        for c, v in w:
            TE.wt(c, v)
        fn(*a, **k).then_inc(tC.h, 1)
        tC.n += 1
        return tC.n

    def V_(fn, *a, w=(), **k):
        VE.wt(vC, vC.n)
        for c, v in w:
            VE.wt(c, v)
        fn(*a, **k).then_inc(vC.h, 1)
        vC.n += 1
        return vC.n

    def S_(fn, *a, w=(), **k):
        SE.wt(sC, sC.n)
        for c, v in w:
            SE.wt(c, v)
        fn(*a, **k).then_inc(sC.h, 1)
        sC.n += 1
        return sC.n

    def SD(ctr, dst, src, w=(), **k):
        for c, v in w:
            SY.wt(c, v)
        nc.sync.dma_start(dst, src, **k).then_inc(ctr.h, 16)
        ctr.n += 16
        return ctr.n

    def GD(ctr, dst, src, w=(), **k):
        for c, v in w:
            GE.wt(c, v)
        nc.gpsimd.dma_start(dst, src, **k).then_inc(ctr.h, 16)
        ctr.n += 16
        return ctr.n

    def barrier():
        for E in ALL_E:
            for c in dma_ctrs:
                E.wt(c, c.n)
            E.e.sem_inc(bar.h, 1)
            bar.n += 1
        for E in ALL_E:
            E.wt(bar, bar.n)

    def sb(stk, name, shape, dt):
        return stk.enter_context(nc.sbuf_tensor(name, shape, dt))

    def pt(stk, name, shape, dt):
        return stk.enter_context(nc.psum_tensor(name, shape, dt))

    mm = nc.tensor.matmul
    tp = nc.tensor.transpose
    act = nc.scalar.activation
    tcp = nc.vector.tensor_copy
    tt_ = nc.vector.tensor_tensor
    ts_ = nc.vector.tensor_scalar
    stt = nc.vector.scalar_tensor_tensor

    eye_bf = sb(st, "eye_bf", [128, 128], BF16)
    bq_sb = sb(st, "bq_sb", [128, 24], F32)
    b1_sb = sb(st, "b1_sb", [128, NE * 32], F32)
    b2_bf = sb(st, "b2_bf", [8, D], BF16)
    bv_bf = sb(st, "bv_bf", [128, 8], BF16)
    iota8 = sb(st, "iota8", [128, 8], F32)
    beff_bc = sb(st, "beff_bc", [128, D], F32)
    w_rows = sb(st, "w_rows", [8, T], BF16)
    ones1 = sb(st, "ones1", [1, 128], BF16)

    pABC = ExitStack()
    kt_full = sb(pABC, "kt_full", [128, 8, T], BF16)
    pAB = ExitStack()
    b2_f = sb(pAB, "b2_f", [8, D], F32)
    wq_bf = sb(pAB, "wq_bf", [128, 8, 3 * D], BF16)
    wq_stage = [sb(pAB, f"wqs{i}", [128, 3 * D], F32) for i in range(2)]
    xnT = [sb(pAB, f"xnT{i}", [128, 8, 512], BF16) for i in range(2)]
    x_t = [sb(pAB, f"x_t{i}", [128, D], F32) for i in range(2)]
    xn_bf = [sb(pAB, f"xn_bf{i}", [128, D], BF16) for i in range(2)]
    mean_t = [sb(pAB, f"mean_{i}", [128, 1], F32) for i in range(2)]
    sq_ab = sb(pAB, "sq_ab", [128, 1024], F32)
    rstd = [sb(pAB, f"rstd{i}", [128, 1], F32) for i in range(2)]
    negmb = [sb(pAB, f"negmb{i}", [128, 1], F32) for i in range(2)]
    qev = [sb(pAB, f"qev{i}", [128, 512], BF16) for i in range(2)]
    vev = [sb(pAB, f"vev{i}", [128, 512], BF16) for i in range(2)]
    ps_q = [pt(pAB, f"ps_q{i}", [128, 512], F32) for i in range(4)]
    ps_t2 = [pt(pAB, f"ps_t2{i}", [128, 512], BF16) for i in range(2)]

    SD(CL, eye_bf[:, :], eye_in[:, :])
    SD(CL, bq_sb[:, :], bqkv_in.rearrange("(c p) -> p c", p=128),
       allow_slow_non_contiguous=True)
    SD(CL, b2_f[:, :], b2_in[:, :])
    for e in range(NE):
        SD(CL, b1_sb[:, e * 32:(e + 1) * 32],
           b1_in[e].rearrange("(c p) -> p c", p=128),
           allow_slow_non_contiguous=True)
    c_lw = (CL, CL.n)
    V_(nc.vector.memset, ones1[:, :], 1.0)
    V_(tcp, b2_bf[:, :], b2_f[:, :], w=[c_lw])
    V_(tcp, bv_bf[:, :], bq_sb[:, 16:24], w=[c_lw])
    nc.gpsimd.iota(iota8[:, :], pattern=[[1, 8]], base=0, channel_multiplier=0,
                   allow_small_or_imprecise_dtypes=True).then_inc(bar.h, 1)
    bar.n += 1
    iota_done = bar.n
    wq_cast = [0, 0]
    for kc in range(8):
        j = kc % 2
        SD(WQ[j], wq_stage[j][:, :], wqkvT_in[kc * 128:(kc + 1) * 128, :],
           w=[(vC, wq_cast[j])])
        wq_cast[j] = V_(tcp, wq_bf[:, kc, :], wq_stage[j][:, :],
                        w=[(WQ[j], WQ[j].n)])
    wq_ready = max(wq_cast)

    xn_done = [0, 0]
    bn_done = [0, 0]
    xnbf_free = [0, 0]
    xnt_ev = [0, 0]
    xnt_free = [0, 0]
    psq_free_c = [None] * 4
    pst_free = [0, 0]
    qev_free = [0, 0]
    vev_free = [0, 0]
    kt_ev = 0

    for s in range(NS_AB):
        sp = s % 2
        for tt in range(4):
            gt = 4 * s + tt
            p = gt % 2
            SD(XA[p], x_t[p][:, :], x_in[gt * 128:(gt + 1) * 128, :],
               w=[(sC, xn_done[p]), (vC, bn_done[p])])
            VE.wt(XA[p], XA[p].n)
            VE.wt(sC, xn_done[p])
            V_(nc.vector.reduce_sum, mean_t[p][:, :], x_t[p][:, :], axis=AX.X)
            V_(ts_, mean_t[p][:, :], mean_t[p][:, :], 1.0 / D, None,
               op0=ALU.mult)
            V_(tt_, sq_ab[:, :], x_t[p][:, :], x_t[p][:, :], op=ALU.mult)
            V_(nc.vector.reduce_sum, rstd[p][:, :], sq_ab[:, :], axis=AX.X)
            bn_done[p] = vC.n
            V_(ts_, rstd[p][:, :], rstd[p][:, :], 1.0 / D, None, op0=ALU.mult)
            V_(tt_, negmb[p][:, :], mean_t[p][:, :], mean_t[p][:, :],
               op=ALU.mult)
            V_(tt_, rstd[p][:, :], rstd[p][:, :], negmb[p][:, :],
               op=ALU.subtract)
            V_(ts_, rstd[p][:, :], rstd[p][:, :], EPS, None, op0=ALU.add)
            v1 = V_(nc.vector.reciprocal, rstd[p][:, :], rstd[p][:, :])
            s1 = S_(act, rstd[p][:, :], rstd[p][:, :], AF.Sqrt, w=[(vC, v1)])
            v2 = V_(stt, negmb[p][:, :], mean_t[p][:, :], -1.0, rstd[p][:, :],
                    op0=ALU.mult, op1=ALU.mult, w=[(sC, s1)])
            xn_done[p] = S_(act, xn_bf[p][:, :], x_t[p][:, :], AF.Identity,
                            bias=negmb[p][:, 0:1], scale=rstd[p][:, 0:1],
                            w=[(vC, v2), (tC, xnbf_free[p])])
            for half in range(2):
                TE.wt(sC, xn_done[p])
                TE.wt(vC, pst_free[half])
                for j in range(4):
                    kc = half * 4 + j
                    T_(tp, ps_t2[half][:, j * 128:(j + 1) * 128],
                       xn_bf[p][:, kc * 128:(kc + 1) * 128], eye_bf[:, :],
                       w=[c_lw])
                pst_free[half] = V_(
                    tcp,
                    xnT[sp][:, half * 4:half * 4 + 4, tt * 128:(tt + 1) * 128],
                    ps_t2[half][:, :].rearrange("p (k c) -> p k c", c=128),
                    w=[(tC, tC.n), (tC, xnt_free[sp])])
            xnbf_free[p] = tC.n
            if dbg == "tile0" and gt == 0:
                dstk = ExitStack()
                tf0 = sb(dstk, "tf0", [128, 1024], F32)
                VE.wt(sC, xn_done[0])
                VE.wt(tC, tC.n)
                dv = V_(tcp, tf0[:, :], xn_bf[0][:, :])
                l0 = SD(DBG, out_hbm[0:128, :], tf0[:, :], w=[(vC, dv)])
                dv = V_(tcp, tf0[:, 0:128], xnT[0][:, 0, 0:128],
                        w=[(DBG, l0)])
                l0 = SD(DBG, out_hbm[128:256, 0:128], tf0[:, 0:128],
                        w=[(vC, dv)])
                l0 = SD(DBG, out_hbm[256:384, 0:1], rstd[0][:, :],
                        allow_slow_non_contiguous=True)
                l0 = SD(DBG, out_hbm[256:384, 1:2], mean_t[0][:, :],
                        allow_slow_non_contiguous=True)
                l0 = SD(DBG, out_hbm[256:384, 2:3], negmb[0][:, :],
                        allow_slow_non_contiguous=True)
                VE.wt(DBG, l0)
                dstk.close()
                pAB.close()
                pABC.close()
                st.close()
                return nc
        xnt_ev[sp] = vC.n
        rot = 0
        for fc in range(16):
            pq = ps_q[rot]
            if psq_free_c[rot] is not None:
                TE.wt(*psq_free_c[rot])
            TE.wt(vC, max(wq_ready, xnt_ev[sp]))
            for kc in range(8):
                T_(mm, pq[:, :], wq_bf[:, kc, fc * 128:(fc + 1) * 128],
                   xnT[sp][:, kc, :], start=(kc == 0), stop=(kc == 7))
            mend = tC.n
            if fc < 8:
                ep = fc % 2
                SE.wt(QW[ep], qev_free[ep])
                se = S_(act, qev[ep][:, :], pq[:, :], AF.Identity,
                        bias=bq_sb[:, fc:fc + 1], w=[(tC, mend)])
                for j in range(4):
                    qev_free[ep] = GD(QW[ep], q_scr[4 * s + j, :, fc, :],
                                      qev[ep][:, j * 128:(j + 1) * 128],
                                      w=[(sC, se)])
                psq_free_c[rot] = (sC, se)
            else:
                kt_ev = S_(act, kt_full[:, fc - 8, s * 512:(s + 1) * 512],
                           pq[:, :], AF.Identity, bias=bq_sb[:, fc:fc + 1],
                           w=[(tC, mend)])
                psq_free_c[rot] = (sC, kt_ev)
            rot = (rot + 1) % 4
        for tt in range(4):
            for f2 in range(2):
                pq = ps_q[rot]
                if psq_free_c[rot] is not None:
                    TE.wt(*psq_free_c[rot])
                for kc in range(8):
                    T_(mm, pq[:, :], xnT[sp][:, kc, tt * 128:(tt + 1) * 128],
                       wq_bf[:, kc, 2 * D + f2 * 512:2 * D + (f2 + 1) * 512],
                       start=(kc == 0), stop=(kc == 7))
                mend = tC.n
                ep = (tt * 2 + f2) % 2
                VE.wt(VW[ep], vev_free[ep])
                ve = V_(tcp, vev[ep][:, :], pq[:, :], w=[(tC, mend)])
                vev_free[ep] = GD(
                    VW[ep],
                    v_scr[(4 * s + tt) * 128:(4 * s + tt + 1) * 128,
                          f2 * 512:(f2 + 1) * 512],
                    vev[ep][:, :], w=[(vC, ve)])
                psq_free_c[rot] = (vC, ve)
                rot = (rot + 1) % 4
        xnt_free[sp] = tC.n

    barrier()
    if dbg == "xt":
        last = SD(DBG, out_hbm[0:128, :], x_t[0][:, :])
        last = SD(DBG, out_hbm[128:256, :], x_t[1][:, :])
        last = SD(DBG, out_hbm[256:384, 0:12], bn6[1][:, :],
                  allow_slow_non_contiguous=True)
        last = SD(DBG, out_hbm[256:384, 12:14], bn2[1][:, :],
                  allow_slow_non_contiguous=True)
        VE.wt(DBG, last)
        pAB.close()
        pABC.close()
        st.close()
        return nc
    if dbg == "ab2":
        dstk = ExitStack()
        tf = sb(dstk, "tf", [128, 1024], F32)
        dv = 0
        last = 0
        dv = V_(tcp, tf[:, :], wq_bf[:, 0, 0:1024], w=[(DBG, last)])
        last = SD(DBG, out_hbm[0:128, :], tf[:, :], w=[(vC, dv)])
        dv = V_(tcp, tf[:, 0:512], xnT[(NS_AB - 1) % 2][:, 0, :],
                w=[(DBG, last)])
        last = SD(DBG, out_hbm[128:256, 0:512], tf[:, 0:512], w=[(vC, dv)])
        dv = V_(tcp, tf[:, :], xn_bf[(4 * NS_AB - 1) % 2][:, :],
                w=[(DBG, last)])
        last = SD(DBG, out_hbm[256:384, :], tf[:, :], w=[(vC, dv)])
        last = SD(DBG, out_hbm[384:512, 0:1], rstd[(4 * NS_AB - 1) % 2][:, :],
                  w=[(vC, dv)], allow_slow_non_contiguous=True)
        last = SD(DBG, out_hbm[384:512, 1:2], negmb[(4 * NS_AB - 1) % 2][:, :],
                  allow_slow_non_contiguous=True)

        VE.wt(DBG, last)
        dstk.close()
        pAB.close()
        pABC.close()
        st.close()
        return nc
    pAB.close()

    if dbg == "qkv":
        dstk = ExitStack()
        tbf = sb(dstk, "tbf", [128, 1024], BF16)
        tf = sb(dstk, "tf", [128, 1024], F32)
        last = 0
        dv = 0
        for b in range(8):
            l1 = SD(DBG, tbf[:, :].rearrange("p (a c) -> p a c", c=128),
                    q_scr[b, :, :, :], w=[(vC, dv)])
            dv = V_(tcp, tf[:, :], tbf[:, :], w=[(DBG, l1)])
            last = SD(DBG, out_hbm[b * 128:(b + 1) * 128, :], tf[:, :],
                      w=[(vC, dv)])
        for b in range(8):
            l1 = SD(DBG, tbf[:, :], v_scr[b * 128:(b + 1) * 128, :],
                    w=[(vC, dv)])
            dv = V_(tcp, tf[:, :], tbf[:, :], w=[(DBG, l1)])
            last = SD(DBG, out_hbm[1024 + b * 128:1024 + (b + 1) * 128, :],
                      tf[:, :], w=[(vC, dv)])
        for b in range(8):
            VE.wt(DBG, last)
            dv = V_(tcp, tf[:, :].rearrange("p (a c) -> p a c", c=128),
                    kt_full[:, :, b * 128:(b + 1) * 128], w=[(vC, dv)])
            last = SD(DBG, out_hbm[2048 + b * 128:2048 + (b + 1) * 128, :],
                      tf[:, :], w=[(vC, dv)])
        VE.wt(DBG, last)
        dstk.close()
        pABC.close()
        st.close()
        return nc

    pC = ExitStack()
    v_sb = sb(pC, "v_sb", [128, NBLK, D], BF16)
    wout_bf = sb(pC, "wout_bf", [128, 8, D], BF16)
    wo_stage = sb(pC, "wos", [128, D], F32)
    pn_bf = sb(pC, "pn_bf", [128, T], BF16)
    m8 = sb(pC, "m8", [128, max(NCH, 1)], F32)
    s8 = sb(pC, "s8", [128, max(NCH, 1)], F32)
    negm = sb(pC, "negm", [128, 1], F32)
    ssum = sb(pC, "ssum", [128, 1], F32)
    rs = sb(pC, "rs", [128, 1], F32)
    qt_b = [sb(pC, f"qt_b{i}", [128, 8, 128], BF16) for i in range(2)]
    pnT = [sb(pC, f"pnT{i}", [128, 512], BF16) for i in range(2)]
    aoT_g = sb(pC, "aoT_g", [128, 8, 128 * len(GLB)], BF16)
    aoT = sb(pC, "aoT0", [128, 8, 128], BF16)
    beff_hi = sb(pC, "beff_hi", [1, D], BF16)
    beff_lo = sb(pC, "beff_lo", [1, D], BF16)
    xd_t = sb(pC, "xd_t", [128, D], F32)
    x1_t = [sb(pC, f"x1_t{i}", [128, D], F32) for i in range(2)]
    beff_f = xd_t[0:1, :]
    beff_tmp = x1_t[0][0:1, :]
    bo_row = x1_t[1][0:1, :]
    xn2_bf = sb(pC, "xn2_bf", [128, D], BF16)
    xnT_d = sb(pC, "xnT_d0", [128, 8, 128], BF16)
    bn2d = sb(pC, "bn2d", [128, 2], F32)
    rstd_d = sb(pC, "rstd_d", [128, 1], F32)
    negm_d = sb(pC, "negm_d", [128, 1], F32)
    a2 = sb(pC, "a2", [128, 1], F32)
    nm2 = sb(pC, "nm2", [128, 1], F32)
    s2 = sb(pC, "s2", [128, 1], F32)
    hf = sb(pC, "hf", [128, 1], F32)
    d8 = sb(pC, "d8", [128, 8], F32)
    c8 = sb(pC, "c8", [128, 8], F32)
    w8t = sb(pC, "w8t", [128, 8], F32)
    w8bf = sb(pC, "w8bf", [128, 8], BF16)
    ps_a = [pt(pC, f"ps_a{i}", [128, 512], F32) for i in range(2)]
    ps_t = [pt(pC, f"ps_t{i}", [128, 512], BF16) for i in range(2)]
    ps_av = pt(pC, "ps_av", [128, 256], F32)

    SD(CL, bo_row[0:1, :], bout_in.rearrange("(a d) -> a d", a=1))
    bo_lw = (CL, CL.n)
    for kt_i in range(NBLK):
        SD(VL, v_sb[:, kt_i, :], v_scr[kt_i * 128:(kt_i + 1) * 128, :])
    v_lw = (VL, VL.n)
    wo_ready = 0
    for kc in range(8):
        SD(WO[0], wo_stage[:, :], woutT_in[kc * 128:(kc + 1) * 128, :],
           w=[(vC, wo_ready)])
        wo_ready = V_(tcp, wout_bf[:, kc, :], wo_stage[:, :],
                      w=[(WO[0], WO[0].n)])
    for f2 in range(2):
        for kc in range(8):
            T_(mm, ps_a[f2][0:1, :], bv_bf[:, kc:kc + 1],
               wout_bf[:, kc, f2 * 512:(f2 + 1) * 512],
               start=(kc == 0), stop=(kc == 7), w=[(vC, wo_ready)])
        V_(tt_, beff_f[0:1, f2 * 512:(f2 + 1) * 512], ps_a[f2][0:1, :],
           bo_row[0:1, f2 * 512:(f2 + 1) * 512], op=ALU.add,
           w=[(tC, tC.n), bo_lw])
    V_(tcp, beff_hi[:, :], beff_f[0:1, :])
    V_(tcp, beff_tmp[0:1, :], beff_hi[:, :])
    V_(tt_, beff_tmp[0:1, :], beff_f[0:1, :], beff_tmp[0:1, :],
       op=ALU.subtract)
    vlo = V_(tcp, beff_lo[:, :], beff_tmp[0:1, :])
    for f2 in range(2):
        T_(mm, ps_a[f2][:, :], ones1[0:1, :],
           beff_hi[0:1, f2 * 512:(f2 + 1) * 512], start=True, stop=False,
           w=[(vC, vlo)])
        T_(mm, ps_a[f2][:, :], ones1[0:1, :],
           beff_lo[0:1, f2 * 512:(f2 + 1) * 512], start=False, stop=True)
        V_(tcp, beff_bc[:, f2 * 512:(f2 + 1) * 512], ps_a[f2][:, :],
           w=[(tC, tC.n)])
    beff_done = vC.n

    if dbg == "beff":
        VE.wt(vC, beff_done)
        nc.vector.sem_inc(bar.h, 1)
        bar.n += 1
        SY.wt(bar, bar.n)
        SD(DBG, out_hbm[0:128, :], beff_bc[:, :])
        SD(DBG, out_hbm[128:256, 0:24], bq_sb[:, :])
        last = SD(DBG, out_hbm[256:257, :], bo_row[0:1, :])
        VE.wt(DBG, last)
        pC.close()
        pABC.close()
        st.close()
        return nc

    psa_free = [(vC, beff_done), (vC, beff_done)]
    pst_free2 = [(vC, 0), (vC, 0)]
    psav_free = [(vC, 0), (vC, 0)]
    pnT_free = [(tC, 0), (tC, 0)]
    qt_free = [(tC, 0), (tC, 0)]
    aoT_free = (tC, 0)
    aoT_ev = 0
    xd_free = (vC, vlo)
    x1_free = [0, 0]
    x1_rd = [(sC, 0), (sC, 0)]
    xnd_free = 0
    w_rows_ev = 0
    avp = 0
    score_free = 0
    last_exp = 0
    pn_free_t = 0
    xn2_free_t = 0
    d_small_free = 0
    w8_free_t = 0

    def attn_D_block(b, aoT_ap, ao_ready_v):
        nonlocal w_rows_ev, xn2_free_t, d_small_free, w8_free_t, xnd_free
        nonlocal xd_free
        bp = b % 2
        SD(XD[0], xd_t[:, :], x_in[b * 128:(b + 1) * 128, :],
           w=[xd_free])
        x1p = x1_t[bp]
        for f2 in range(2):
            TE.wt(*psa_free[f2])
            TE.wt(vC, max(wo_ready, ao_ready_v))
            for kc in range(8):
                T_(mm, ps_a[f2][:, :], aoT_ap[:, kc, :],
                   wout_bf[:, kc, f2 * 512:(f2 + 1) * 512],
                   start=(kc == 0), stop=(kc == 7))
        mend = tC.n
        VE.wt(XD[0], XD[0].n)
        VE.wt(X1W[bp], x1_free[bp])
        VE.wt(sC, x1_rd[bp][1])
        for f2 in range(2):
            V_(stt, x1p[:, f2 * 512:(f2 + 1) * 512], ps_a[f2][:, :], 1.0,
               xd_t[:, f2 * 512:(f2 + 1) * 512], op0=ALU.mult, op1=ALU.add,
               w=[(tC, mend), (vC, beff_done)])
            psa_free[f2] = (vC, vC.n)
        xd_free = (vC, vC.n)
        vx1 = V_(tt_, x1p[:, :], x1p[:, :], beff_bc[:, :], op=ALU.add)
        x1_free[bp] = GD(X1W[bp], x1_scr[b * 128:(b + 1) * 128, :], x1p[:, :],
                         w=[(vC, vx1)])
        VE.wt(sC, d_small_free)
        V_(nc.vector.reduce_sum, bn2d[:, 0:1], x1p[:, :], axis=AX.X)
        V_(ts_, bn2d[:, 0:1], bn2d[:, 0:1], 1.0 / D, None, op0=ALU.mult)
        V_(tt_, xd_t[:, :], x1p[:, :], x1p[:, :], op=ALU.mult)
        V_(nc.vector.reduce_sum, rstd_d[:, :], xd_t[:, :], axis=AX.X)
        xd_free = (vC, vC.n)
        V_(ts_, rstd_d[:, :], rstd_d[:, :], 1.0 / D, None, op0=ALU.mult)
        V_(tt_, negm_d[:, :], bn2d[:, 0:1], bn2d[:, 0:1], op=ALU.mult)
        V_(tt_, rstd_d[:, :], rstd_d[:, :], negm_d[:, :], op=ALU.subtract)
        V_(ts_, rstd_d[:, :], rstd_d[:, :], EPS, None, op0=ALU.add)
        v1 = V_(nc.vector.reciprocal, rstd_d[:, :], rstd_d[:, :])
        s1 = S_(act, rstd_d[:, :], rstd_d[:, :], AF.Sqrt, w=[(vC, v1)])
        v2 = V_(stt, negm_d[:, :], bn2d[:, 0:1], -1.0, rstd_d[:, :],
                op0=ALU.mult, op1=ALU.mult, w=[(sC, s1)])
        sxn = S_(act, xn2_bf[:, :], x1p[:, :], AF.Identity,
                 bias=negm_d[:, 0:1], scale=rstd_d[:, 0:1],
                 w=[(vC, v2), (tC, xn2_free_t)])
        d_small_free = sxn
        x1_rd[bp] = (sC, sxn)
        V_(tt_, a2[:, :], x1p[:, 0:1], x1p[:, 1:2], op=ALU.add)
        V_(ts_, nm2[:, :], negm_d[:, :], 2.0, None, op0=ALU.mult)
        V_(stt, s2[:, :], a2[:, :], rstd_d[:, 0:1], nm2[:, :],
           op0=ALU.mult, op1=ALU.add)
        V_(ts_, a2[:, :], s2[:, :], 0.0, None, op0=ALU.is_lt)
        V_(ts_, s2[:, :], s2[:, :], 64.0, None, op0=ALU.add)
        for M in (64.0, 32.0, 16.0, 8.0):
            V_(ts_, hf[:, :], s2[:, :], M, None, op0=ALU.is_ge)
            V_(stt, s2[:, :], hf[:, :], -M, s2[:, :],
               op0=ALU.mult, op1=ALU.add)
        V_(tt_, s2[:, :], s2[:, :], a2[:, :], op=ALU.add)
        V_(ts_, hf[:, :], s2[:, :], 8.0, None, op0=ALU.is_ge)
        V_(stt, s2[:, :], hf[:, :], -8.0, s2[:, :],
           op0=ALU.mult, op1=ALU.add)
        V_(stt, d8[:, :], iota8[:, :], s2[:, 0:1], iota8[:, :],
           op0=ALU.subtract, op1=ALU.bypass, w=[(bar, iota_done)])
        V_(ts_, c8[:, :], d8[:, :], -7.0, None, op0=ALU.is_le)
        V_(ts_, w8t[:, :], d8[:, :], 1.0, None, op0=ALU.is_le)
        V_(ts_, d8[:, :], d8[:, :], -1.0, None, op0=ALU.is_le)
        V_(tt_, w8t[:, :], w8t[:, :], d8[:, :], op=ALU.subtract)
        V_(tt_, w8t[:, :], w8t[:, :], c8[:, :], op=ALU.add)
        V_(ts_, w8t[:, :], w8t[:, :], 0.5, None, op0=ALU.mult)
        vw8 = V_(tcp, w8bf[:, :], w8t[:, :], w=[(tC, w8_free_t)])
        TE.wt(*pst_free2[0])
        tw = T_(tp, ps_t[0][0:8, 0:128], w8bf[:, :], eye_bf[:, :],
                w=[(vC, vw8)])
        w8_free_t = tw
        w_rows_ev = V_(tcp, w_rows[0:8, b * 128:(b + 1) * 128],
                       ps_t[0][0:8, 0:128], w=[(tC, tw)])
        pst_free2[0] = (vC, w_rows_ev)
        VE.wt(XTW[0], xnd_free)
        for half in range(2):
            TE.wt(*pst_free2[half])
            TE.wt(sC, sxn)
            for j in range(4):
                kc = half * 4 + j
                T_(tp, ps_t[half][:, j * 128:(j + 1) * 128],
                   xn2_bf[:, kc * 128:(kc + 1) * 128], eye_bf[:, :])
            ev = V_(tcp, xnT_d[:, half * 4:half * 4 + 4, :],
                    ps_t[half][:, :].rearrange("p (k c) -> p k c", c=128),
                    w=[(tC, tC.n)])
            pst_free2[half] = (vC, ev)
        xn2_free_t = tC.n
        vev_ = vC.n
        for kc in range(8):
            xnd_free = GD(XTW[0], xn2T_scr[kc, :, b * 128:(b + 1) * 128],
                          xnT_d[:, kc, :], w=[(vC, vev_)])

    for gi, b in enumerate(GLB):
        gp = gi % 2
        SY.wt(*qt_free[gp])
        SD(QL[gp], qt_b[gp][:, :, :], q_scr[b, :, :, :])
        q_lw = (QL[gp], QL[gp].n)
        for h in range(NH):
            po = (h % 2) * 64
            hc = h // 2
            lhs_q = qt_b[gp][po:po + 64, hc, :]
            for c in range(NCH):
                pa = ps_a[c % 2]
                TE.wt(*psa_free[c % 2])
                TE.wt(*q_lw)
                TE.wt(sC, kt_ev)
                T_(mm, pa[:, :], lhs_q,
                   kt_full[po:po + 64, hc, c * 512:(c + 1) * 512],
                   start=True, stop=True)
                mx = V_(nc.vector.reduce_max, m8[:, c:c + 1], pa[:, :],
                        axis=AX.X, negate=False, w=[(tC, tC.n)])
                psa_free[c % 2] = (vC, mx)
            V_(nc.vector.reduce_max, negm[:, :], m8[:, 0:NCH], axis=AX.X,
               negate=True, w=[(sC, last_exp)])
            vn = V_(ts_, negm[:, :], negm[:, :], 0.125, None, op0=ALU.mult)
            SE.wt(tC, pn_free_t)
            for c in range(NCH):
                pa = ps_a[c % 2]
                TE.wt(*psa_free[c % 2])
                T_(mm, pa[:, :], lhs_q,
                   kt_full[po:po + 64, hc, c * 512:(c + 1) * 512],
                   start=True, stop=True)
                ex = S_(act, pn_bf[:, c * 512:(c + 1) * 512], pa[:, :], AF.Exp,
                        bias=negm[:, 0:1], scale=0.125,
                        accum_out=s8[:, c:c + 1],
                        w=[(vC, vn), (tC, tC.n)])
                psa_free[c % 2] = (sC, ex)
            send = sC.n
            last_exp = send
            V_(nc.vector.reduce_sum, ssum[:, :], s8[:, 0:NCH], axis=AX.X,
               w=[(sC, send)])
            V_(nc.vector.reciprocal, rs[:, :], ssum[:, :])
            vnorm = V_(stt, pn_bf[:, 0:T], pn_bf[:, 0:T], rs[:, 0:1],
                       pn_bf[:, 0:T], op0=ALU.mult, op1=ALU.bypass)
            for g4 in range(NBLK // 4):
                tpar = g4 % 2
                TE.wt(*pst_free2[tpar])
                TE.wt(vC, vnorm)
                for j in range(4):
                    kt_i = g4 * 4 + j
                    T_(tp, ps_t[tpar][:, j * 128:(j + 1) * 128],
                       pn_bf[:, kt_i * 128:(kt_i + 1) * 128], eye_bf[:, :])
                VE.wt(*pnT_free[tpar])
                ev = V_(tcp, pnT[tpar][:, :], ps_t[tpar][:, :], w=[(tC, tC.n)])
                pst_free2[tpar] = (vC, ev)
                TE.wt(vC, ev)
                if g4 == 0:
                    TE.wt(*psav_free[avp])
                    TE.wt(*v_lw)
                for j in range(4):
                    kt_i = g4 * 4 + j
                    T_(mm, ps_av[po:po + 64, avp * 128:avp * 128 + 128],
                       v_sb[:, kt_i, h * 64:(h + 1) * 64],
                       pnT[tpar][:, j * 128:(j + 1) * 128],
                       start=(kt_i == 0), stop=(kt_i == NBLK - 1))
                pnT_free[tpar] = (tC, tC.n)
            pn_free_t = tC.n
            aoT_ev = V_(tcp, aoT_g[po:po + 64, hc, gi * 128:(gi + 1) * 128],
                        ps_av[po:po + 64, avp * 128:avp * 128 + 128],
                        w=[(tC, tC.n)])
            psav_free[avp] = (vC, aoT_ev)
            avp ^= 1
        qt_free[gp] = (tC, tC.n)

    for b in range(NBLK):
        if b not in GLB:
            KBL = GLB + [b]
            NW = 128 * len(KBL)
            bp2 = b % 2
            SY.wt(*qt_free[bp2])
            SD(QL[bp2], qt_b[bp2][:, :, :], q_scr[b, :, :, :])
            q_lw = (QL[bp2], QL[bp2].n)
            for h in range(NH):
                po = (h % 2) * 64
                hc = h // 2
                sp2 = (b * NH + h) % 2
                pa = ps_a[sp2]
                TE.wt(*psa_free[sp2])
                TE.wt(*q_lw)
                TE.wt(sC, kt_ev)
                for j, kb in enumerate(KBL):
                    T_(mm, pa[:, j * 128:(j + 1) * 128],
                       qt_b[bp2][po:po + 64, hc, :],
                       kt_full[po:po + 64, hc, kb * 128:(kb + 1) * 128],
                       start=True, stop=True)
                mend = tC.n
                V_(nc.vector.reduce_max, negm[:, :], pa[:, 0:NW], axis=AX.X,
                   negate=True, w=[(tC, mend), (sC, last_exp)])
                vn = V_(ts_, negm[:, :], negm[:, :], 0.125, None, op0=ALU.mult)
                se_ = S_(act, pn_bf[:, 0:NW], pa[:, 0:NW], AF.Exp,
                         bias=negm[:, 0:1], scale=0.125,
                         accum_out=ssum[:, :],
                         w=[(vC, vn), (tC, pn_free_t)])
                last_exp = se_
                psa_free[sp2] = (sC, se_)
                V_(nc.vector.reciprocal, rs[:, :], ssum[:, :], w=[(sC, se_)])
                vnorm = V_(stt, pn_bf[:, 0:NW], pn_bf[:, 0:NW], rs[:, 0:1],
                           pn_bf[:, 0:NW], op0=ALU.mult, op1=ALU.bypass)
                tpar = sp2
                TE.wt(*pst_free2[tpar])
                TE.wt(vC, vnorm)
                for j in range(len(KBL)):
                    T_(tp, ps_t[tpar][:, j * 128:(j + 1) * 128],
                       pn_bf[:, j * 128:(j + 1) * 128], eye_bf[:, :])
                VE.wt(*pnT_free[tpar])
                ev = V_(tcp, pnT[tpar][:, 0:NW], ps_t[tpar][:, 0:NW],
                        w=[(tC, tC.n)])
                pst_free2[tpar] = (vC, ev)
                TE.wt(vC, ev)
                TE.wt(*psav_free[avp])
                TE.wt(*v_lw)
                for j, kb in enumerate(KBL):
                    T_(mm, ps_av[po:po + 64, avp * 128:avp * 128 + 128],
                       v_sb[:, kb, h * 64:(h + 1) * 64],
                       pnT[tpar][:, j * 128:(j + 1) * 128],
                       start=(j == 0), stop=(j == len(KBL) - 1))
                pnT_free[tpar] = (tC, tC.n)
                pn_free_t = tC.n
                VE.wt(*aoT_free)
                aoT_ev = V_(tcp, aoT[po:po + 64, hc, :],
                            ps_av[po:po + 64, avp * 128:avp * 128 + 128],
                            w=[(tC, tC.n)])
                psav_free[avp] = (vC, aoT_ev)
                avp ^= 1
            qt_free[bp2] = (tC, tC.n)
            attn_D_block(b, aoT[:, :, :], aoT_ev)
            aoT_free = (tC, tC.n)
        else:
            gi = GLB.index(b)
            attn_D_block(b, aoT_g[:, :, gi * 128:(gi + 1) * 128], aoT_ev)

    barrier()
    pC.close()
    pABC.close()

    if dbg == "x1":
        last = 0
        for b in range(NBLK):
            last = SD(DBG, out_hbm[b * 128:(b + 1) * 128, :],
                      x1_scr[b * 128:(b + 1) * 128, :])
        VE.wt(DBG, last)
        st.close()
        return nc
    if dbg == "xn2T":
        last = 0
        for kc in range(8):
            last = SD(DBG, out_hbm[kc * 512:kc * 512 + 128, 0:T // 4],
                      xn2T_scr[kc, :, 0:T // 4])
        VE.wt(DBG, last)
        st.close()
        return nc

    pM = ExitStack()
    w1_bf = sb(pM, "w1_bf", [128, 8, DFF], BF16)
    e8hot = sb(pM, "e8hot_sb", [8, 8 * 128], BF16)
    w2_bf = sb(pM, "w2_bf", [128, 32, D], BF16)
    stg = [sb(pM, f"stg{i}", [128, 1024], F32) for i in range(2)]
    H_bf = sb(pM, "H_bf", [128, 32, 256], BF16)
    relu_t = [sb(pM, f"relu{i}", [128, 256], F32) for i in range(2)]
    xnT_s = [sb(pM, f"xnTs{i}", [128, 8, 256], BF16) for i in range(2)]
    acc_in = sb(pM, "acc_in", [128, 2, D], F32)
    x1_in = sb(pM, "x1_in", [128, 2, D], F32)
    out_w = sb(pM, "out_w", [128, 2, D], F32)
    ps_h = [pt(pM, f"ps_h{i}", [128, 256], F32) for i in range(2)]
    ps_o = [pt(pM, f"ps_o{i}", [128, 512], F32) for i in range(3)]
    ps_w = pt(pM, "ps_w", [128, 256], F32)

    stg_free = [(vC, 0), (vC, 0)]
    xnt_lds = [0, 0]
    xnt_frees = [(tC, 0), (tC, 0)]
    psh_free = [(sC, 0), (sC, 0)]
    pso_free = [(vC, 0), (vC, 0), (vC, 0)]
    psw_free = (vC, 0)
    relu_free = [(vC, 0), (vC, 0)]
    h_end = 0
    o_end = 0
    hbf_read = 0
    accin_free = (vC, 0)
    x1in_free = (vC, 0)
    outw_free = [0, 0]
    acc_wr = {}

    def xnt_load(s):
        par = s % 2
        SY.wt(*xnt_frees[par])
        for kc in range(8):
            SD(XS[par], xnT_s[par][:, kc, :],
               xn2T_scr[kc, :, s * 256:(s + 1) * 256])
        xnt_lds[par] = XS[par].n

    SD(CL, e8hot[:, :], e8h_in[:, :])
    e8_lw = (CL, CL.n)
    xnt_load(0)
    for e in range(NE):
        for kc in range(8):
            for hl in range(4):
                j = (kc * 4 + hl) % 2
                SD(STG[j], stg[j][:, :],
                   W1_in[e, kc * 128:(kc + 1) * 128,
                         hl * 1024:(hl + 1) * 1024],
                   w=[stg_free[j]])
                V_(tcp, w1_bf[:, kc, hl * 1024:(hl + 1) * 1024], stg[j][:, :],
                   w=[(STG[j], STG[j].n), (tC, h_end)])
                stg_free[j] = (vC, vC.n)
        w1_ready = vC.n
        for m in range(32):
            j = m % 2
            SD(STG[j], stg[j][:, :], W2_in[e, m * 128:(m + 1) * 128, :],
               w=[stg_free[j]])
            S_(act, w2_bf[:, m, :], stg[j][:, :], AF.Identity,
               w=[(STG[j], STG[j].n), (tC, o_end)])
            stg_free[j] = (sC, sC.n)
        w2_ready = sC.n

        for s in range(NS_MOE):
            sp = s % 2
            if s + 1 < NS_MOE:
                xnt_load(s + 1)
            elif e + 1 < NE:
                xnt_load(0)
            if e >= 1:
                GE.wt(*accin_free)
                GE.wt(OW[0], acc_wr[s][0])
                GE.wt(OW[1], acc_wr[s][1])
                GD(AL, acc_in[:, 0, :], acc_scr[s * 256:s * 256 + 128, :])
                GD(AL, acc_in[:, 1, :], acc_scr[s * 256 + 128:(s + 1) * 256, :])
                acc_ld = AL.n
            if e == NE - 1:
                SY.wt(*x1in_free)
                SD(XLL, x1_in[:, 0, :], x1_scr[s * 256:s * 256 + 128, :])
                SD(XLL, x1_in[:, 1, :], x1_scr[s * 256 + 128:(s + 1) * 256, :])
                x1_ld = XLL.n
            TE.wt(*psw_free)
            T_(mm, ps_w[:, :], e8hot[:, e * 128:(e + 1) * 128],
               w_rows[:, s * 256:(s + 1) * 256], start=True, stop=True,
               w=[(vC, w_rows_ev), e8_lw])
            wbc_mm = tC.n
            TE.wt(vC, w1_ready)
            TE.wt(XS[sp], xnt_lds[sp])
            for m in range(32):
                hp = m % 2
                TE.wt(*psh_free[hp])
                for kc in range(8):
                    T_(mm, ps_h[hp][:, :], w1_bf[:, kc, m * 128:(m + 1) * 128],
                       xnT_s[sp][:, kc, :], start=(kc == 0), stop=(kc == 7))
                mend = tC.n
                SE.wt(*relu_free[hp])
                rl = S_(act, relu_t[hp][:, :], ps_h[hp][:, :], AF.Relu,
                        bias=b1_sb[:, e * 32 + m:e * 32 + m + 1],
                        w=[(tC, mend)])
                psh_free[hp] = (sC, rl)
                if m == 0:
                    VE.wt(tC, max(wbc_mm, hbf_read))
                sc = V_(tt_, H_bf[:, m, :], relu_t[hp][:, :], ps_w[:, :],
                        op=ALU.mult, w=[(sC, rl)])
                relu_free[hp] = (vC, sc)
            scale_end = vC.n
            psw_free = (vC, scale_end)
            xnt_frees[sp] = (tC, tC.n)
            if s == NS_MOE - 1:
                h_end = tC.n
            TE.wt(vC, scale_end)
            TE.wt(sC, w2_ready)
            rot3 = 0
            ev_cnt = [0, 0]
            for ttk in range(2):
                for fc in range(2):
                    po_ = ps_o[rot3]
                    TE.wt(*pso_free[rot3])
                    for m in range(32):
                        T_(mm, po_[:, :], H_bf[:, m, ttk * 128:(ttk + 1) * 128],
                           w2_bf[:, m, fc * 512:(fc + 1) * 512],
                           start=(m == 0),
                           stop=(m == 31 and e != 0))
                    if e == 0:
                        T_(mm, po_[:, :],
                           w_rows[:, s * 256 + ttk * 128:s * 256 + (ttk + 1) * 128],
                           b2_bf[:, fc * 512:(fc + 1) * 512],
                           start=False, stop=True)
                    mend = tC.n
                    VE.wt(tC, mend)
                    VE.wt(OW[ttk], outw_free[ttk])
                    dst = out_w[:, ttk, fc * 512:(fc + 1) * 512]
                    if e == 0:
                        V_(tcp, dst, po_[:, :])
                    elif e < NE - 1:
                        VE.wt(AL, acc_ld)
                        V_(tt_, dst, po_[:, :],
                           acc_in[:, ttk, fc * 512:(fc + 1) * 512], op=ALU.add)
                    else:
                        VE.wt(AL, acc_ld)
                        VE.wt(XLL, x1_ld)
                        V_(tt_, dst, po_[:, :],
                           acc_in[:, ttk, fc * 512:(fc + 1) * 512], op=ALU.add)
                        V_(tt_, dst, dst, x1_in[:, ttk, fc * 512:(fc + 1) * 512],
                           op=ALU.add)
                    pso_free[rot3] = (vC, vC.n)
                    ev_cnt[ttk] = vC.n
                    rot3 = (rot3 + 1) % 3
            if s == NS_MOE - 1:
                o_end = tC.n
            hbf_read = tC.n
            if e >= 1:
                accin_free = (vC, max(ev_cnt))
            if e == NE - 1:
                x1in_free = (vC, max(ev_cnt))
            tgt = out_hbm if e == NE - 1 else acc_scr
            for ttk in range(2):
                outw_free[ttk] = GD(
                    OW[ttk],
                    tgt[s * 256 + ttk * 128:s * 256 + (ttk + 1) * 128, :],
                    out_w[:, ttk, :], w=[(vC, ev_cnt[ttk])])
            acc_wr[s] = (OW[0].n, OW[1].n)

    VE.wt(OW[0], OW[0].n)
    VE.wt(OW[1], OW[1].n)
    pM.close()
    st.close()
    return nc


def kernel(L0_attn_x, L0_attn_w_qkv, L0_attn_b_qkv, L0_attn_w_out, L0_attn_b_out,
           L0_ffn_W1, L0_ffn_b1, L0_ffn_W2, L0_ffn_b2):
    x = np.ascontiguousarray(np.asarray(L0_attn_x, dtype=np.float32))
    wqkvT = np.ascontiguousarray(np.asarray(L0_attn_w_qkv, dtype=np.float32).T)
    woutT = np.ascontiguousarray(np.asarray(L0_attn_w_out, dtype=np.float32).T)
    bqkv = np.ascontiguousarray(np.asarray(L0_attn_b_qkv, dtype=np.float32))
    bout = np.ascontiguousarray(np.asarray(L0_attn_b_out, dtype=np.float32))
    W1 = np.ascontiguousarray(np.asarray(L0_ffn_W1, dtype=np.float32))
    b1 = np.ascontiguousarray(np.asarray(L0_ffn_b1, dtype=np.float32))
    W2 = np.ascontiguousarray(np.asarray(L0_ffn_W2, dtype=np.float32))
    b2 = np.ascontiguousarray(np.asarray(L0_ffn_b2, dtype=np.float32))
    eye = np.eye(128, dtype=BFNP)
    e8hot = np.zeros((8, 8 * 128), dtype=BFNP)
    for e in range(NE):
        e8hot[e, e * 128:(e + 1) * 128] = 1

    nc = build(T=4096)
    in_maps = [{"x": x[i], "wqkvT": wqkvT, "bqkv": bqkv, "woutT": woutT,
                "bout": bout, "W1": W1, "b1": b1, "W2": W2, "b2": b2,
                "eye": eye, "e8hot": e8hot} for i in range(8)]
    res = run_bass_kernel_spmd(nc, in_maps, core_ids=list(range(8)))
    return np.stack([res.results[i]["out"] for i in range(8)], axis=0)
